# version 07
# baseline (speedup 1.0000x reference)
import numpy as np
import ml_dtypes
import concourse.bass as bass
import concourse.mybir as mybir
from concourse.bass_utils import run_bass_kernel_spmd

F32 = mybir.dt.float32
BF16 = mybir.dt.bfloat16
AF = mybir.ActivationFunctionType
AL = mybir.AluOpType
AX = mybir.AxisListType

D = 2048
KC = 16
INTER = 1408
IC = 11
E_R = 16
E_ALL = 18
T_OWN = 1024
HALO = 128
T_FULL = 1280
RB_FULL = 10
RB_OWN = 8
NH = 16
HD = 128
KW = 384
QSCALE = float(HD) ** -0.5
NEG = -1e30

ENGN = ["sync", "scalar", "vector", "tensor", "gpsimd"]


class Ctx:

    def __init__(self, me):
        self.me = me
        self.eng = None
        self.sems = None
        self.cnt = {}
        self.slot_last = [None, None, None, None]
        self.rmw_idx = 0

    def val(self, s):
        return self.cnt.get(s, 0)

    def bump(self, s, n):
        self.cnt[s] = self.cnt.get(s, 0) + n
        return self.cnt[s]

    def do(self, ename, fn, inc=None, inc_n=1, waits=()):
        if self.me == ename:
            for s, v in waits:
                if v > 0:
                    self.eng.wait_ge(self.sems[s], v)
            inst = fn(self.eng)
            if inc is not None:
                inst.then_inc(self.sems[inc], inc_n)
        if inc is not None:
            self.bump(inc, inc_n)

    def drain(self, ename):
        if self.me == ename:
            self.eng.drain()

    def wait(self, ename, s, v):
        if self.me == ename and v > 0:
            self.eng.wait_ge(self.sems[s], v)


SEM_NAMES = (
    ["stg", "wA", "wB", "wq0", "wq1", "wg", "aw0", "aw1", "wo", "mk",
     "sv", "ss", "st", "sg", "idl", "li0", "li1", "fin"]
    + [f"xi{r}" for r in range(RB_FULL)]
    + [f"xo{r}" for r in range(RB_FULL)]
)


def build_nc(n_sub=3):
    nc = bass.Bass("TRN2", target_bir_lowering=False)

    x_in = nc.dram_tensor("x", [T_FULL, D], F32, kind="ExternalInput")
    ident_in = nc.dram_tensor("identb", [128, 128], BF16, kind="ExternalInput")
    masks_in = nc.dram_tensor("masks", [128, 8 * KW], F32, kind="ExternalInput")
    wq_in = nc.dram_tensor("Wq", [D, D], F32, kind="ExternalInput")
    wk_in = nc.dram_tensor("Wk", [D, D], F32, kind="ExternalInput")
    wv_in = nc.dram_tensor("Wv", [D, D], F32, kind="ExternalInput")
    wo_in = nc.dram_tensor("Wo", [D, D], F32, kind="ExternalInput")
    wg_in = nc.dram_tensor("Wg", [E_R, D], F32, kind="ExternalInput")
    w1_in = nc.dram_tensor("W1", [E_R * INTER, D], F32, kind="ExternalInput")
    w3_in = nc.dram_tensor("W3", [E_R * INTER, D], F32, kind="ExternalInput")
    w2_in = nc.dram_tensor("W2", [E_R * D, INTER], F32, kind="ExternalInput")
    ws1_in = nc.dram_tensor("Ws1", [2 * INTER, D], F32, kind="ExternalInput")
    ws3_in = nc.dram_tensor("Ws3", [2 * INTER, D], F32, kind="ExternalInput")
    ws2_in = nc.dram_tensor("Ws2", [D, 2 * INTER], F32, kind="ExternalInput")
    y_out = nc.dram_tensor("y", [T_OWN, D], F32, kind="ExternalOutput")

    X = nc.dram_tensor("Xres", [T_FULL, D], F32)
    w1s = nc.dram_tensor("w1s", [E_R * INTER, D], BF16)
    w3s = nc.dram_tensor("w3s", [E_R * INTER, D], BF16)
    w2s = nc.dram_tensor("w2s", [E_R * D, INTER], BF16)
    ws1s = nc.dram_tensor("ws1s", [2 * INTER, D], BF16)
    ws3s = nc.dram_tensor("ws3s", [2 * INTER, D], BF16)
    ws2s = nc.dram_tensor("ws2s", [D, 2 * INTER], BF16)
    wgs = nc.dram_tensor("wgs", [E_R, D], BF16)

    sem_ctx = []
    sems = {}
    for s in SEM_NAMES:
        cm = nc.semaphore(s)
        sems[s] = cm.__enter__()
        sem_ctx.append(cm)

    ctxs = {e: Ctx(e) for e in ENGN}
    for c in ctxs.values():
        c.sems = sems

    def run_phase(script):
        with nc.Block() as block:
            for ename in ENGN:
                def make(en):
                    def f(eng):
                        ctxs[en].eng = eng
                        script(ctxs[en])
                    return f
                getattr(block, ename)(make(ename))

    pers_ctx = []

    def sb(name, shape, dtype):
        cm = nc.sbuf_tensor(name, shape, dtype)
        t = cm.__enter__()
        pers_ctx.append(cm)
        return t

    xhT = sb("xhT", [128, KC, T_FULL], BF16)
    comb = sb("comb", [128, RB_FULL, E_R], F32)
    wgT = sb("wgT", [128, KC, E_R], BF16)
    ident = sb("ident", [128, 128], BF16)
    stats = sb("stats", [128, 8], F32)
    xq = sb("xq", [128, 4, 512], F32)

    stg_ready = {}

    def script_prologue(c):
        c.do("sync", lambda s: s.dma_start(ident[:, :], ident_in[:, :]),
             inc="idl", inc_n=16)
        c.do("gpsimd", lambda g: g.dma_start(X[:, :], x_in[:, :]), inc="stg", inc_n=16)
        stg_ready["Xinit"] = c.val("stg")

    def script_stage0(c):
        casts = [("wg", wgs[:, :], wg_in[:, :])]
        for e in range(E_R):
            casts.append((f"w1_{e}", w1s[e * INTER:(e + 1) * INTER, :], w1_in[e * INTER:(e + 1) * INTER, :]))
            casts.append((f"w3_{e}", w3s[e * INTER:(e + 1) * INTER, :], w3_in[e * INTER:(e + 1) * INTER, :]))
            casts.append((f"w2_{e}", w2s[e * D:(e + 1) * D, :], w2_in[e * D:(e + 1) * D, :]))
        casts.append(("ws1", ws1s[:, :], ws1_in[:, :]))
        casts.append(("ws3", ws3s[:, :], ws3_in[:, :]))
        casts.append(("ws2", ws2s[:, :], ws2_in[:, :]))
        for name, dst, src in casts:
            c.do("gpsimd", lambda g, d=dst, s=src: g.dma_start(d, s),
                 inc="stg", inc_n=16, waits=[("stg", c.val("stg"))])
            stg_ready[name] = c.val("stg")
        for k in range(KC):
            c.do("sync", lambda s, k=k: s.dma_start(
                wgT[:, k, :], wgs[:, k * 128:(k + 1) * 128], transpose=True),
                inc="wg", inc_n=16, waits=[("stg", stg_ready["wg"])] if k == 0 else ())

    def script_ln(c, ln_tensors, rbs, toff):
        xbuf, sq, xh_tm, ps_t = ln_tensors
        for i, rb in enumerate(rbs):
            ring = i % 2
            li = "li0" if ring == 0 else "li1"
            waits = [("stg", stg_ready["Xinit"]),
                     (f"xo{rb}", c.val(f"xo{rb}")),
                     ("sv", c.val("sv") if i >= 2 else 0)]
            if i == 0:
                waits += [(f"xo{r}", c.val(f"xo{r}")) for r in range(RB_FULL) if r != rb]
            c.do("sync", lambda s, rb=rb, ring=ring: s.dma_start(
                xbuf[:, ring, :], X[rb * 128:(rb + 1) * 128, :]),
                inc=li, inc_n=16, waits=waits)
            liv = c.val(li)
            c.do("vector", lambda v, ring=ring: v.reduce_sum(
                stats[:, 0:1], xbuf[:, ring, :], AX.X), waits=[(li, liv)])
            c.drain("vector")
            c.do("vector", lambda v: v.tensor_scalar(
                stats[:, 0:1], stats[:, 0:1], 1.0 / D, None, AL.mult))
            c.drain("vector")
            c.do("vector", lambda v, ring=ring: v.tensor_scalar(
                xbuf[:, ring, :], xbuf[:, ring, :], stats[:, 0:1], None, AL.subtract),
                inc="sv")
            sv_xc = c.val("sv")
            c.do("scalar", lambda s, ring=ring: s.square(sq[:, :], xbuf[:, ring, :]),
                 inc="ss", waits=[("sv", sv_xc)])
            ss_sq = c.val("ss")
            c.do("vector", lambda v: v.reduce_sum(stats[:, 1:2], sq[:, :], AX.X),
                 waits=[("ss", ss_sq)])
            c.drain("vector")
            c.do("vector", lambda v: v.tensor_scalar(
                stats[:, 1:2], stats[:, 1:2], 1.0 / D, 1e-5, AL.mult, AL.add), inc="sv")
            sv_var = c.val("sv")
            c.do("scalar", lambda s: s.sqrt(stats[:, 2:3], stats[:, 1:2]),
                 inc="ss", waits=[("sv", sv_var)])
            ss_sd = c.val("ss")
            c.do("vector", lambda v: v.reciprocal(stats[:, 3:4], stats[:, 2:3]),
                 waits=[("ss", ss_sd)])
            c.drain("vector")
            c.do("vector", lambda v, ring=ring: v.tensor_scalar(
                xh_tm[:, :], xbuf[:, ring, :], stats[:, 3:4], None, AL.mult), inc="sv")
            sv_xh = c.val("sv")
            for k in range(KC):
                pp = k % 2
                c.do("tensor", lambda t, k=k, pp=pp: t.transpose(
                    ps_t[:, pp, 0:128], xh_tm[:, k * 128:(k + 1) * 128], ident[:, :]),
                    inc="st",
                    waits=[("sv", sv_xh if k < 2 else c.val("sv")),
                           ("idl", 16 if (i == 0 and k == 0) else 0)])
                stv = c.val("st")
                tcol = toff + i * 128
                c.do("vector", lambda v, k=k, pp=pp, tcol=tcol: v.tensor_copy(
                    xhT[:, k, tcol:tcol + 128], ps_t[:, pp, 0:128]), inc="sv",
                    waits=[("st", stv)])

    def script_router(c, rt, n_rb):
        ps_r, sc, gm, t8, msk = rt
        for i in range(n_rb):
            if i == 0:
                c.do("vector", lambda v: v.memset(gm[:, 4:8], NEG))
                c.drain("vector")
            for k in range(KC):
                c.do("tensor", lambda t, k=k, i=i: t.matmul(
                    ps_r[:, 0:E_R], xhT[:, k, i * 128:(i + 1) * 128], wgT[:, k, :],
                    start=(k == 0), stop=(k == KC - 1)),
                    inc="st" if k == KC - 1 else None,
                    waits=([("wg", 16 * KC), ("sv", c.val("sv"))] if k == 0 else ()))
            stv = c.val("st")
            c.do("vector", lambda v: v.reduce_max(stats[:, 4:5], ps_r[:, 0:E_R], AX.X,
                                                  negate=True), waits=[("st", stv)])
            c.drain("vector")
            c.do("vector", lambda v: v.tensor_copy(sc[:, :], ps_r[:, 0:E_R]), inc="sv")
            c.do("scalar", lambda s: s.activation(sc[:, :], sc[:, :], AF.Exp,
                                                  bias=stats[:, 4:5]),
                 inc="ss", waits=[("sv", c.val("sv"))])
            c.do("vector", lambda v: v.reduce_sum(stats[:, 5:6], sc[:, :], AX.X),
                 waits=[("ss", c.val("ss"))])
            c.drain("vector")
            c.do("vector", lambda v: v.reciprocal(stats[:, 6:7], stats[:, 5:6]))
            c.drain("vector")
            c.do("vector", lambda v: v.tensor_scalar(
                sc[:, :], sc[:, :], stats[:, 6:7], None, AL.mult))
            c.drain("vector")
            c.do("vector", lambda v: v.tensor_tensor(gm[:, 0:4], sc[:, 0:16:4], sc[:, 1:16:4], AL.max))
            c.drain("vector")
            c.do("vector", lambda v: v.tensor_tensor(gm[:, 0:4], gm[:, 0:4], sc[:, 2:16:4], AL.max))
            c.drain("vector")
            c.do("vector", lambda v: v.tensor_tensor(gm[:, 0:4], gm[:, 0:4], sc[:, 3:16:4], AL.max))
            c.drain("vector")
            c.do("vector", lambda v: v.max(out=t8[:, :], in_=gm[:, 0:8]))
            c.drain("vector")
            c.do("vector", lambda v: v.tensor_scalar(gm[:, 0:4], gm[:, 0:4], t8[:, 1:2], None, AL.is_ge))
            c.drain("vector")
            c.do("vector", lambda v: v.tensor_scalar(gm[:, 0:4], gm[:, 0:4], 1e30, -1e30, AL.mult, AL.add))
            c.drain("vector")
            for g in range(4):
                c.do("vector", lambda v, g=g: v.tensor_scalar(
                    msk[:, 4 * g:4 * g + 4], sc[:, 4 * g:4 * g + 4], gm[:, g:g + 1], None, AL.add))
            c.drain("vector")
            c.do("vector", lambda v: v.max(out=t8[:, :], in_=msk[:, :]))
            c.drain("vector")
            c.do("vector", lambda v: v.tensor_scalar(msk[:, :], msk[:, :], t8[:, 3:4], None, AL.is_ge))
            c.drain("vector")
            c.do("vector", lambda v, i=i: v.scalar_tensor_tensor(
                comb[:, i, :], sc[:, :], 0.5, msk[:, :], AL.mult, AL.mult), inc="sv")

    def emit_xin(c, rb, dq, dst_slot, q="sync"):
        waits = [("stg", stg_ready["Xinit"]), (f"xo{rb}", c.val(f"xo{rb}"))]
        if c.slot_last[dst_slot] is not None:
            waits.append(c.slot_last[dst_slot])
        c.do(q, lambda s, rb=rb, dq=dq, sl=dst_slot: s.dma_start(
            xq[:, sl, :], X[rb * 128:(rb + 1) * 128, dq * 512:(dq + 1) * 512]),
            inc=f"xi{rb}", inc_n=16, waits=waits)
        return c.val(f"xi{rb}")

    def script_expert(c, ft, e, nchunks, rbs, last_ffn, war_st, xin_q="sync"):
        (w1T, w3T, w2q, h, s1, ps_h1, ps_h3, ps_y) = ft
        t_tot = 128 * len(rbs)
        if e < E_R:
            src1, src3 = w1s[e * INTER:(e + 1) * INTER, :], w3s[e * INTER:(e + 1) * INTER, :]
            stg_gate = max(stg_ready[f"w1_{e}"], stg_ready[f"w3_{e}"])
        else:
            j = e - E_R
            src1, src3 = ws1s[j * INTER:(j + 1) * INTER, :], ws3s[j * INTER:(j + 1) * INTER, :]
            stg_gate = max(stg_ready["ws1"], stg_ready["ws3"])
        for k in range(KC):
            c.do("sync", lambda s, k=k, src=src1: s.dma_start(
                w1T[:, k, :], src[:, k * 128:(k + 1) * 128], transpose=True),
                inc="wA", inc_n=16,
                waits=[("stg", stg_gate), ("st", war_st)] if k == 0 else ())
            c.do("sync", lambda s, k=k, src=src3: s.dma_start(
                w3T[:, k, :], src[:, k * 128:(k + 1) * 128], transpose=True),
                inc="wB", inc_n=16)
        wA_ready, wB_ready = c.val("wA"), c.val("wB")

        silu_done = [c.val("ss"), c.val("ss")]
        hmul_done = [c.val("sv"), c.val("sv")]
        it = 0
        for ici in range(IC):
            for (t0, n) in nchunks:
                par = it % 2
                for k in range(KC):
                    waits = []
                    if ici == 0 and t0 == 0 and k == 0:
                        waits = [("wA", wA_ready), ("wB", wB_ready), ("sv", c.val("sv"))]
                    if k == 0:
                        waits.append(("ss", silu_done[par]))
                    c.do("tensor", lambda t, k=k, ici=ici, t0=t0, n=n, par=par: t.matmul(
                        ps_h1[:, par, 0:n], w1T[:, k, ici * 128:(ici + 1) * 128],
                        xhT[:, k, t0:t0 + n], start=(k == 0), stop=(k == KC - 1)),
                        inc="st" if k == KC - 1 else None, waits=waits)
                st_h1 = c.val("st")
                c.do("scalar", lambda s, n=n, par=par: s.activation(
                    s1[:, par, 0:n], ps_h1[:, par, 0:n], AF.Silu), inc="ss",
                    waits=[("st", st_h1), ("sv", hmul_done[par])])
                silu_done[par] = c.val("ss")
                for k in range(KC):
                    waits = [("sv", hmul_done[par])] if k == 0 else []
                    c.do("tensor", lambda t, k=k, ici=ici, t0=t0, n=n, par=par: t.matmul(
                        ps_h3[:, par, 0:n], w3T[:, k, ici * 128:(ici + 1) * 128],
                        xhT[:, k, t0:t0 + n], start=(k == 0), stop=(k == KC - 1)),
                        inc="st" if k == KC - 1 else None, waits=waits)
                st_h3 = c.val("st")
                c.do("vector", lambda v, ici=ici, t0=t0, n=n, par=par: v.tensor_tensor(
                    h[:, ici, t0:t0 + n], s1[:, par, 0:n], ps_h3[:, par, 0:n], AL.mult),
                    inc="sv", waits=[("st", st_h3), ("ss", silu_done[par])])
                hmul_done[par] = c.val("sv")
                it += 1
        st_h_done = c.val("st")
        sv_h_done = c.val("sv")

        if e < E_R:
            w2src = w2s[e * D:(e + 1) * D, :]
            stg2 = stg_ready[f"w2_{e}"]
        else:
            j = e - E_R
            w2src = ws2s[:, j * INTER:(j + 1) * INTER]
            stg2 = stg_ready["ws2"]
        wq_war = [c.val("st"), c.val("st")]
        for dq in range(4):
            ring = dq % 2
            wqs = f"wq{ring}"
            for ici in range(IC):
                c.do("sync", lambda s, dq=dq, ici=ici, ring=ring, w2src=w2src: s.dma_start(
                    w2q[:, ring, ici, :],
                    w2src[dq * 512:(dq + 1) * 512, ici * 128:(ici + 1) * 128],
                    transpose=True),
                    inc=wqs, inc_n=16,
                    waits=[("stg", stg2), ("st", wq_war[ring])] if ici == 0 else ())
            wq_ready = c.val(wqs)
            for i, rb in enumerate(rbs):
                slot = c.rmw_idx % 4
                xi_v = emit_xin(c, rb, dq, slot, q=xin_q)
                pp = i % 2
                for ici in range(IC):
                    waits = []
                    if ici == 0:
                        waits = [(wqs, wq_ready),
                                 ("sv", c.val("sv") if (dq > 0 or i >= 2) else sv_h_done)]
                    c.do("tensor", lambda t, ici=ici, i=i, ring=ring, pp=pp: t.matmul(
                        ps_y[:, pp, :], h[:, ici, i * 128:(i + 1) * 128],
                        w2q[:, ring, ici, :], start=(ici == 0), stop=(ici == IC - 1)),
                        inc="st" if ici == IC - 1 else None, waits=waits)
                st_y = c.val("st")
                if e < E_R:
                    c.do("vector", lambda v, i=i, e=e, slot=slot, pp=pp: v.scalar_tensor_tensor(
                        xq[:, slot, :], ps_y[:, pp, :], comb[:, i, e:e + 1],
                        xq[:, slot, :], AL.mult, AL.add),
                        inc="sv", waits=[("st", st_y), (f"xi{rb}", xi_v)])
                else:
                    c.do("vector", lambda v, slot=slot, pp=pp: v.scalar_tensor_tensor(
                        xq[:, slot, :], ps_y[:, pp, :], 0.5,
                        xq[:, slot, :], AL.mult, AL.add),
                        inc="sv", waits=[("st", st_y), (f"xi{rb}", xi_v)])
                sv_y = c.val("sv")
                if last_ffn and e == E_ALL - 1:
                    dst = y_out[(rb - 1) * 128:rb * 128, dq * 512:(dq + 1) * 512]
                else:
                    dst = X[rb * 128:(rb + 1) * 128, dq * 512:(dq + 1) * 512]
                c.do("sync", lambda s, dst=dst, slot=slot: s.dma_start(dst, xq[:, slot, :]),
                     inc=f"xo{rb}", inc_n=16, waits=[("sv", sv_y)])
                c.slot_last[slot] = (f"xo{rb}", c.val(f"xo{rb}"))
                c.rmw_idx += 1
            wq_war[ring] = c.val("st")
        return st_h_done

    def script_ffn(c, ft, rt, nchunks, rbs, last_ffn, xin_q="sync"):
        script_router(c, rt, len(rbs))
        war = [c.val("st")]
        for e in range(E_ALL):
            war[0] = script_expert(c, ft, e, nchunks, rbs, last_ffn, war[0], xin_q)

    def script_attn(c, at):
        (wo_sb, qkv, qT, kT, vTt, vtm, amask, attnT, ssc, psm, pbf,
         ps_q, ps_s, ps_pt, ps_vt, ps_o, ps_y) = at
        entry_st = c.val("st")
        entry_sv = c.val("sv")
        c.do("scalar", lambda s: s.dma_start(
            amask[:, :, :], masks_in.rearrange("p (q w) -> p q w", q=8)),
            inc="mk", inc_n=16, waits=[("st", entry_st), ("sv", entry_sv)])
        for k in range(KC):
            c.do("gpsimd", lambda g, k=k: g.dma_start(
                wo_sb[:, k, :], wo_in[k * 128:(k + 1) * 128, :]),
                inc="wo", inc_n=16,
                waits=[("stg", c.val("stg")), ("st", entry_st), ("sv", entry_sv)] if k == 0 else ())
        wo_ready = c.val("wo")
        war_qkv = [entry_st, entry_st]
        st_attn_done_by_head = []
        for hh in range(NH):
            ring = hh % 2
            aws = f"aw{ring}"
            for wi, wsrc in enumerate((wq_in, wk_in, wv_in)):
                c.do("gpsimd", lambda g, wi=wi, wsrc=wsrc, hh=hh, ring=ring: g.dma_start(
                    qkv[:, ring, wi, :, :],
                    wsrc.rearrange("(k p) f -> p k f", p=128)[:, :, hh * 128:(hh + 1) * 128]),
                    inc=aws, inc_n=16,
                    waits=[("st", war_qkv[ring])] if wi == 0 else ())
            aw_ready = c.val(aws)
            def proj(widx, dst, chunks, toff_src, scale, sv_prev):
                for (t0, n) in chunks:
                    for k in range(KC):
                        waits = []
                        if k == 0:
                            waits = [(aws, aw_ready), ("ss", c.val("ss"))]
                        c.do("tensor", lambda t, k=k, widx=widx, t0=t0, n=n, ring=ring: t.matmul(
                            ps_q[:, 0:n], qkv[:, ring, widx, k, :],
                            xhT[:, k, toff_src + t0: toff_src + t0 + n],
                            start=(k == 0), stop=(k == KC - 1)),
                            inc="st" if k == KC - 1 else None, waits=waits)
                    stv = c.val("st")
                    if scale is None:
                        c.do("scalar", lambda s, t0=t0, n=n, dst=dst: s.copy(
                            dst[:, t0:t0 + n], ps_q[:, 0:n]), inc="ss", waits=[("st", stv)])
                    else:
                        c.do("scalar", lambda s, t0=t0, n=n, dst=dst: s.activation(
                            dst[:, t0:t0 + n], ps_q[:, 0:n], AF.Copy, scale=scale),
                            inc="ss", waits=[("st", stv)])
            proj(0, qT, [(0, 512), (512, 512)], HALO, QSCALE, 0)
            proj(1, kT, [(0, 512), (512, 512), (1024, 256)], 0, None, 0)
            proj(2, vTt, [(0, 512), (512, 512), (1024, 256)], 0, None, 0)
            ss_proj = c.val("ss")
            for tv in range(RB_FULL):
                pp = tv % 2
                c.do("tensor", lambda t, tv=tv, pp=pp: t.transpose(
                    ps_vt[:, pp, 0:128], vTt[:, tv * 128:(tv + 1) * 128], ident[:, :]),
                    inc="st",
                    waits=[("ss", ss_proj), ("sv", c.val("sv"))] if tv == 0 else
                          ([("sv", c.val("sv"))] if tv >= 2 else []))
                stv = c.val("st")
                c.do("vector", lambda v, tv=tv, pp=pp: v.tensor_copy(
                    vtm[:, tv, :], ps_vt[:, pp, 0:128]), inc="sv", waits=[("st", stv)])
            for qc in range(RB_OWN):
                c.do("tensor", lambda t, qc=qc: t.matmul(
                    ps_s[:, 0:KW], qT[:, qc * 128:(qc + 1) * 128],
                    kT[:, qc * 128:qc * 128 + KW], start=True, stop=True),
                    inc="st", waits=[("sv", c.val("sv")), ("mk", 16)])
                st_s = c.val("st")
                c.do("vector", lambda v, qc=qc: v.tensor_tensor(
                    ssc[:, :], ps_s[:, 0:KW], amask[:, qc, :], AL.add),
                    waits=[("st", st_s), ("ss", c.val("ss"))])
                c.drain("vector")
                c.do("vector", lambda v: v.reduce_max(stats[:, 4:5], ssc[:, :], AX.X,
                                                      negate=True), inc="sv")
                c.do("scalar", lambda s: s.activation(psm[:, :], ssc[:, :], AF.Exp,
                                                      bias=stats[:, 4:5]),
                     inc="ss", waits=[("sv", c.val("sv"))])
                c.do("vector", lambda v: v.reduce_sum(stats[:, 5:6], psm[:, :], AX.X),
                     waits=[("ss", c.val("ss"))])
                c.drain("vector")
                c.do("vector", lambda v: v.reciprocal(stats[:, 6:7], stats[:, 5:6]))
                c.drain("vector")
                c.do("vector", lambda v: v.tensor_scalar(
                    pbf[:, :], psm[:, :], stats[:, 6:7], None, AL.mult), inc="sv")
                sv_pb = c.val("sv")
                for j in range(3):
                    c.do("tensor", lambda t, j=j: t.transpose(
                        ps_pt[:, j * 128:(j + 1) * 128], pbf[:, j * 128:(j + 1) * 128],
                        ident[:, :]), inc="st",
                        waits=[("sv", sv_pb)] if j == 0 else [])
                st_pt = c.val("st")
                c.do("vector", lambda v: v.tensor_copy(pbf[:, :], ps_pt[:, 0:KW]),
                     inc="sv", waits=[("st", st_pt)])
                sv_pt = c.val("sv")
                for j in range(3):
                    c.do("tensor", lambda t, j=j, qc=qc: t.matmul(
                        ps_o[:, :], vtm[:, qc + j, :], pbf[:, j * 128:(j + 1) * 128],
                        start=(j == 0), stop=(j == 2)),
                        inc="st" if j == 2 else None,
                        waits=[("sv", sv_pt), ("ss", c.val("ss"))] if j == 0 else [])
                st_o = c.val("st")
                c.do("scalar", lambda s, hh=hh, qc=qc: s.copy(
                    attnT[:, hh, qc * 128:(qc + 1) * 128], ps_o[:, :]),
                    inc="ss", waits=[("st", st_o)])
            war_qkv[ring] = c.val("st")
            st_attn_done_by_head.append(c.val("st"))
        ss_attn_done = c.val("ss")
        for dq in range(4):
            for i in range(RB_OWN):
                rb = i + 1
                slot = c.rmw_idx % 4
                xi_v = emit_xin(c, rb, dq, slot)
                pp = i % 2
                for k in range(KC):
                    waits = []
                    if k == 0:
                        waits = [("wo", wo_ready), ("ss", ss_attn_done),
                                 ("sv", c.val("sv") if (dq > 0 or i >= 2) else 0)]
                    c.do("tensor", lambda t, k=k, i=i, dq=dq, pp=pp: t.matmul(
                        ps_y[:, pp, :], attnT[:, k, i * 128:(i + 1) * 128],
                        wo_sb[:, k, dq * 512:(dq + 1) * 512],
                        start=(k == 0), stop=(k == KC - 1)),
                        inc="st" if k == KC - 1 else None, waits=waits)
                st_y = c.val("st")
                c.do("vector", lambda v, slot=slot, pp=pp: v.scalar_tensor_tensor(
                    xq[:, slot, :], ps_y[:, pp, :], 1.0, xq[:, slot, :],
                    AL.mult, AL.add), inc="sv", waits=[("st", st_y), (f"xi{rb}", xi_v)])
                sv_y = c.val("sv")
                c.do("sync", lambda s, rb=rb, dq=dq, slot=slot: s.dma_start(
                    X[rb * 128:(rb + 1) * 128, dq * 512:(dq + 1) * 512], xq[:, slot, :]),
                    inc=f"xo{rb}", inc_n=16, waits=[("sv", sv_y)])
                c.slot_last[slot] = (f"xo{rb}", c.val(f"xo{rb}"))
                c.rmw_idx += 1

    def script_dump_X(c):
        c.do("sync", lambda s: s.dma_start(y_out[:, :], X[HALO:HALO + T_OWN, :]),
             inc="fin", inc_n=16,
             waits=[(f"xo{r}", c.val(f"xo{r}")) for r in range(RB_FULL)])
        c.wait("sync", "fin", c.val("fin"))

    def script_finish(c):
        for r in range(1, 1 + RB_OWN):
            c.wait("sync", f"xo{r}", c.val(f"xo{r}"))

    run_phase(script_prologue)

    ln_ctx = []

    def alloc(name, shape, dtype):
        cm = nc.sbuf_tensor(name, shape, dtype)
        t = cm.__enter__()
        ln_ctx.append((cm, t))
        return t

    def alloc_ps(name, shape, dtype):
        cm = nc.psum_tensor(name, shape, dtype)
        t = cm.__enter__()
        ln_ctx.append((cm, t))
        return t

    def free_all():
        while ln_ctx:
            cm, _ = ln_ctx.pop()
            cm.__exit__(None, None, None)

    def ln_alloc(tag):
        return (alloc(f"xbuf{tag}", [128, 2, D], F32),
                alloc(f"sq{tag}", [128, D], F32),
                alloc(f"xh_tm{tag}", [128, D], BF16),
                alloc_ps(f"ps_t{tag}", [128, 2, 1024], BF16))

    def ffn_alloc(tag):
        ft = (alloc(f"w1T{tag}", [128, KC, INTER], BF16),
              alloc(f"w3T{tag}", [128, KC, INTER], BF16),
              alloc(f"w2q{tag}", [128, 2, IC, 512], BF16),
              alloc(f"h{tag}", [128, IC, T_FULL], BF16),
              alloc(f"s1{tag}", [128, 2, 512], F32),
              alloc_ps(f"ps_h1{tag}", [128, 2, 512], F32),
              alloc_ps(f"ps_h3{tag}", [128, 2, 512], F32),
              alloc_ps(f"ps_y{tag}", [128, 2, 512], F32))
        rt = (alloc_ps(f"ps_r{tag}", [128, 512], F32),
              alloc(f"rsc{tag}", [128, E_R], F32),
              alloc(f"rgm{tag}", [128, 8], F32),
              alloc(f"rt8{tag}", [128, 8], F32),
              alloc(f"rmsk{tag}", [128, E_R], F32))
        return ft, rt

    t_ln = ln_alloc("a")
    run_phase(lambda c: script_ln(c, t_ln, list(range(RB_FULL)), 0))
    free_all()
    ft, rt = ffn_alloc("a")

    def phase_ffn1(c):
        script_stage0(c)
        script_ffn(c, ft, rt, [(0, 512), (512, 512), (1024, 256)],
                   list(range(RB_FULL)), last_ffn=False, xin_q="scalar")
    run_phase(phase_ffn1)
    free_all()
    if n_sub == 1:
        run_phase(script_dump_X)
        for cm in reversed(sem_ctx):
            cm.__exit__(None, None, None)
        for cm in reversed(pers_ctx):
            cm.__exit__(None, None, None)
        return nc

    t_ln = ln_alloc("b")
    run_phase(lambda c: script_ln(c, t_ln, list(range(RB_FULL)), 0))
    free_all()
    at = (alloc("wo_sb", [128, KC, D], BF16),
          alloc("qkv", [128, 2, 3, KC, 128], BF16),
          alloc("qT", [128, T_OWN], BF16),
          alloc("kT", [128, T_FULL], BF16),
          alloc("vTt", [128, T_FULL], BF16),
          alloc("vtm", [128, RB_FULL, 128], BF16),
          alloc("amask", [128, 8, KW], F32),
          alloc("attnT", [128, NH, T_OWN], BF16),
          alloc("ssc", [128, KW], F32),
          alloc("psm", [128, KW], F32),
          alloc("pbf", [128, KW], BF16),
          alloc_ps("aps_q", [128, 512], F32),
          alloc_ps("aps_s", [128, 512], F32),
          alloc_ps("aps_pt", [128, KW], BF16),
          alloc_ps("aps_vt", [128, 2, 1024], BF16),
          alloc_ps("aps_o", [128, 128], F32),
          alloc_ps("aps_y", [128, 2, 512], F32))
    run_phase(lambda c: script_attn(c, at))
    free_all()
    if n_sub == 2:
        run_phase(script_dump_X)
        for cm in reversed(sem_ctx):
            cm.__exit__(None, None, None)
        for cm in reversed(pers_ctx):
            cm.__exit__(None, None, None)
        return nc

    t_ln = ln_alloc("d")
    run_phase(lambda c: script_ln(c, t_ln, list(range(1, 1 + RB_OWN)), 0))
    free_all()
    ft, rt = ffn_alloc("d")
    run_phase(lambda c: script_ffn(c, ft, rt, [(0, 512), (512, 512)],
                                   list(range(1, 1 + RB_OWN)), last_ffn=True,
                                   xin_q="scalar"))
    free_all()
    run_phase(script_finish)

    for cm in reversed(sem_ctx):
        cm.__exit__(None, None, None)
    for cm in reversed(pers_ctx):
        cm.__exit__(None, None, None)
    return nc


def _build_mask(qstart):
    S = 4096
    m = np.full((8, 128, KW), NEG, dtype=np.float32)
    for qc in range(8):
        gq = qstart + qc * 128 + np.arange(128)
        gk = qstart - HALO + qc * 128 + np.arange(KW)
        valid = (gk >= 0) & (gk < S)
        band = np.abs(gq[:, None] // 64 - gk[None, :] // 64) <= 2
        m[qc] = np.where(valid[None, :] & band, 0.0, NEG)
    return np.ascontiguousarray(m.transpose(1, 0, 2).reshape(128, 8 * KW))


def kernel(L0_attn_hidden_states, L0_attn_Wq, L0_attn_Wk, L0_attn_Wv, L0_attn_Wo,
           L0_ffn_Wg, L0_ffn_W1, L0_ffn_W3, L0_ffn_W2, L0_ffn_Ws1, L0_ffn_Ws3,
           L0_ffn_Ws2, _n_sub=3):
    x = np.asarray(L0_attn_hidden_states, dtype=np.float32)
    B, S, _ = x.shape
    w = {
        "Wq": np.ascontiguousarray(L0_attn_Wq, dtype=np.float32),
        "Wk": np.ascontiguousarray(L0_attn_Wk, dtype=np.float32),
        "Wv": np.ascontiguousarray(L0_attn_Wv, dtype=np.float32),
        "Wo": np.ascontiguousarray(L0_attn_Wo, dtype=np.float32),
        "Wg": np.ascontiguousarray(L0_ffn_Wg, dtype=np.float32),
        "W1": np.ascontiguousarray(np.asarray(L0_ffn_W1, dtype=np.float32).reshape(E_R * INTER, D)),
        "W3": np.ascontiguousarray(np.asarray(L0_ffn_W3, dtype=np.float32).reshape(E_R * INTER, D)),
        "W2": np.ascontiguousarray(np.asarray(L0_ffn_W2, dtype=np.float32).reshape(E_R * D, INTER)),
        "Ws1": np.ascontiguousarray(L0_ffn_Ws1, dtype=np.float32),
        "Ws3": np.ascontiguousarray(L0_ffn_Ws3, dtype=np.float32),
        "Ws2": np.ascontiguousarray(L0_ffn_Ws2, dtype=np.float32),
    }
    in_maps = []
    for core in range(8):
        b, q = core // 4, core % 4
        qstart = q * T_OWN
        xs = np.zeros((T_FULL, D), dtype=np.float32)
        lo, hi = qstart - HALO, qstart + T_OWN + HALO
        clo, chi = max(lo, 0), min(hi, S)
        xs[clo - lo:chi - lo] = x[b, clo:chi]
        m = dict(w)
        m["x"] = xs
        m["masks"] = _build_mask(qstart)
        m["identb"] = np.eye(128, dtype=ml_dtypes.bfloat16)
        in_maps.append(m)
    nc = build_nc(n_sub=_n_sub)
    res = run_bass_kernel_spmd(nc, in_maps, core_ids=list(range(8)))
    out = np.empty((B, S, D), dtype=np.float32)
    for core in range(8):
        b, q = core // 4, core % 4
        out[b, q * T_OWN:(q + 1) * T_OWN] = res.results[core]["y"]
    return out


# version 08
# speedup vs baseline: 1.1338x; 1.1338x over previous
import numpy as np
import ml_dtypes
import concourse.bass as bass
import concourse.mybir as mybir
from concourse.bass_utils import run_bass_kernel_spmd

F32 = mybir.dt.float32
BF16 = mybir.dt.bfloat16
AF = mybir.ActivationFunctionType
AL = mybir.AluOpType
AX = mybir.AxisListType

D = 2048
KC = 16
INTER = 1408
IC = 11
E_R = 16
E_ALL = 18
T_OWN = 1024
HALO = 128
T_FULL = 1280
RB_FULL = 10
RB_OWN = 8
NH = 16
HD = 128
KW = 384
QSCALE = float(HD) ** -0.5
NEG = -1e30

ENGN = ["sync", "scalar", "vector", "tensor", "gpsimd"]


class Ctx:

    def __init__(self, me):
        self.me = me
        self.eng = None
        self.sems = None
        self.cnt = {}
        self.slot_last = [None, None, None, None]
        self.rmw_idx = 0

    def val(self, s):
        return self.cnt.get(s, 0)

    def bump(self, s, n):
        self.cnt[s] = self.cnt.get(s, 0) + n
        return self.cnt[s]

    def do(self, ename, fn, inc=None, inc_n=1, waits=()):
        if self.me == ename:
            for s, v in waits:
                if v > 0:
                    self.eng.wait_ge(self.sems[s], v)
            inst = fn(self.eng)
            if inc is not None:
                inst.then_inc(self.sems[inc], inc_n)
        if inc is not None:
            self.bump(inc, inc_n)

    def drain(self, ename):
        if self.me == ename:
            self.eng.drain()

    def wait(self, ename, s, v):
        if self.me == ename and v > 0:
            self.eng.wait_ge(self.sems[s], v)


SEM_NAMES = (
    ["stg", "wA", "wB", "wq0", "wq1", "wg", "aw0", "aw1", "wo", "mk",
     "sv", "ss", "st", "sg", "idl", "li0", "li1", "fin"]
    + [f"xi{r}" for r in range(RB_FULL)]
    + [f"xo{r}" for r in range(RB_FULL)]
)


def build_nc(n_sub=3):
    nc = bass.Bass("TRN2", target_bir_lowering=False)

    x_in = nc.dram_tensor("x", [T_FULL, D], F32, kind="ExternalInput")
    ident_in = nc.dram_tensor("identb", [128, 128], BF16, kind="ExternalInput")
    masks_in = nc.dram_tensor("masks", [128, 8 * KW], F32, kind="ExternalInput")
    wq_in = nc.dram_tensor("Wq", [D, D], F32, kind="ExternalInput")
    wk_in = nc.dram_tensor("Wk", [D, D], F32, kind="ExternalInput")
    wv_in = nc.dram_tensor("Wv", [D, D], F32, kind="ExternalInput")
    wo_in = nc.dram_tensor("Wo", [D, D], F32, kind="ExternalInput")
    wg_in = nc.dram_tensor("Wg", [E_R, D], F32, kind="ExternalInput")
    w1_in = nc.dram_tensor("W1", [E_R * INTER, D], F32, kind="ExternalInput")
    w3_in = nc.dram_tensor("W3", [E_R * INTER, D], F32, kind="ExternalInput")
    w2_in = nc.dram_tensor("W2", [E_R * D, INTER], F32, kind="ExternalInput")
    ws1_in = nc.dram_tensor("Ws1", [2 * INTER, D], F32, kind="ExternalInput")
    ws3_in = nc.dram_tensor("Ws3", [2 * INTER, D], F32, kind="ExternalInput")
    ws2_in = nc.dram_tensor("Ws2", [D, 2 * INTER], F32, kind="ExternalInput")
    y_out = nc.dram_tensor("y", [T_OWN, D], F32, kind="ExternalOutput")

    X = nc.dram_tensor("Xres", [T_FULL, D], F32)
    w1s = nc.dram_tensor("w1s", [E_R * INTER, D], BF16)
    w3s = nc.dram_tensor("w3s", [E_R * INTER, D], BF16)
    w2s = nc.dram_tensor("w2s", [E_R * D, INTER], BF16)
    ws1s = nc.dram_tensor("ws1s", [2 * INTER, D], BF16)
    ws3s = nc.dram_tensor("ws3s", [2 * INTER, D], BF16)
    ws2s = nc.dram_tensor("ws2s", [D, 2 * INTER], BF16)
    wgs = nc.dram_tensor("wgs", [E_R, D], BF16)

    sem_ctx = []
    sems = {}
    for s in SEM_NAMES:
        cm = nc.semaphore(s)
        sems[s] = cm.__enter__()
        sem_ctx.append(cm)

    ctxs = {e: Ctx(e) for e in ENGN}
    for c in ctxs.values():
        c.sems = sems

    def run_phase(script):
        with nc.Block() as block:
            for ename in ENGN:
                def make(en):
                    def f(eng):
                        ctxs[en].eng = eng
                        script(ctxs[en])
                    return f
                getattr(block, ename)(make(ename))

    pers_ctx = []

    def sb(name, shape, dtype):
        cm = nc.sbuf_tensor(name, shape, dtype)
        t = cm.__enter__()
        pers_ctx.append(cm)
        return t

    xhT = sb("xhT", [128, KC, T_FULL], BF16)
    comb = sb("comb", [128, RB_FULL, E_R], F32)
    wgT = sb("wgT", [128, KC, E_R], BF16)
    ident = sb("ident", [128, 128], BF16)
    stats = sb("stats", [128, 8], F32)
    xq = sb("xq", [128, 4, 512], F32)

    stg_ready = {}

    def script_prologue(c):
        c.do("sync", lambda s: s.dma_start(ident[:, :], ident_in[:, :]),
             inc="idl", inc_n=16)
        c.do("gpsimd", lambda g: g.dma_start(X[:, :], x_in[:, :]), inc="stg", inc_n=16)
        stg_ready["Xinit"] = c.val("stg")
        c.do("gpsimd", lambda g: g.dma_start(wgs[:, :], wg_in[:, :]),
             inc="stg", inc_n=16, waits=[("stg", c.val("stg"))])
        stg_ready["wg"] = c.val("stg")


    def script_ln(c, ln_tensors, rbs, toff):
        xbuf, sq, xh_tm, ps_t = ln_tensors
        for i, rb in enumerate(rbs):
            ring = i % 2
            li = "li0" if ring == 0 else "li1"
            waits = [("stg", stg_ready["Xinit"]),
                     (f"xo{rb}", c.val(f"xo{rb}")),
                     ("sv", c.val("sv") if i >= 2 else 0)]
            if i == 0:
                waits += [(f"xo{r}", c.val(f"xo{r}")) for r in range(RB_FULL) if r != rb]
            c.do("sync", lambda s, rb=rb, ring=ring: s.dma_start(
                xbuf[:, ring, :], X[rb * 128:(rb + 1) * 128, :]),
                inc=li, inc_n=16, waits=waits)
            liv = c.val(li)
            c.do("vector", lambda v, ring=ring: v.reduce_sum(
                stats[:, 0:1], xbuf[:, ring, :], AX.X), waits=[(li, liv)])
            c.drain("vector")
            c.do("vector", lambda v: v.tensor_scalar(
                stats[:, 0:1], stats[:, 0:1], 1.0 / D, None, AL.mult))
            c.drain("vector")
            c.do("vector", lambda v, ring=ring: v.tensor_scalar(
                xbuf[:, ring, :], xbuf[:, ring, :], stats[:, 0:1], None, AL.subtract),
                inc="sv")
            sv_xc = c.val("sv")
            c.do("scalar", lambda s, ring=ring: s.square(sq[:, :], xbuf[:, ring, :]),
                 inc="ss", waits=[("sv", sv_xc)])
            ss_sq = c.val("ss")
            c.do("vector", lambda v: v.reduce_sum(stats[:, 1:2], sq[:, :], AX.X),
                 waits=[("ss", ss_sq)])
            c.drain("vector")
            c.do("vector", lambda v: v.tensor_scalar(
                stats[:, 1:2], stats[:, 1:2], 1.0 / D, 1e-5, AL.mult, AL.add), inc="sv")
            sv_var = c.val("sv")
            c.do("scalar", lambda s: s.sqrt(stats[:, 2:3], stats[:, 1:2]),
                 inc="ss", waits=[("sv", sv_var)])
            ss_sd = c.val("ss")
            c.do("vector", lambda v: v.reciprocal(stats[:, 3:4], stats[:, 2:3]),
                 waits=[("ss", ss_sd)])
            c.drain("vector")
            c.do("vector", lambda v, ring=ring: v.tensor_scalar(
                xh_tm[:, :], xbuf[:, ring, :], stats[:, 3:4], None, AL.mult), inc="sv")
            sv_xh = c.val("sv")
            for k in range(KC):
                pp = k % 2
                c.do("tensor", lambda t, k=k, pp=pp: t.transpose(
                    ps_t[:, pp, 0:128], xh_tm[:, k * 128:(k + 1) * 128], ident[:, :]),
                    inc="st",
                    waits=[("sv", sv_xh if k < 2 else c.val("sv")),
                           ("idl", 16 if (i == 0 and k == 0) else 0)])
                stv = c.val("st")
                tcol = toff + i * 128
                c.do("vector", lambda v, k=k, pp=pp, tcol=tcol: v.tensor_copy(
                    xhT[:, k, tcol:tcol + 128], ps_t[:, pp, 0:128]), inc="sv",
                    waits=[("st", stv)])

    def script_router(c, rt, n_rb):
        ps_r, sc, gm, t8, msk = rt
        for i in range(n_rb):
            if i == 0:
                c.do("vector", lambda v: v.memset(gm[:, 4:8], NEG))
                c.drain("vector")
            for k in range(KC):
                c.do("tensor", lambda t, k=k, i=i: t.matmul(
                    ps_r[:, 0:E_R], xhT[:, k, i * 128:(i + 1) * 128], wgT[:, k, :],
                    start=(k == 0), stop=(k == KC - 1)),
                    inc="st" if k == KC - 1 else None,
                    waits=([("wg", 16 * KC), ("sv", c.val("sv"))] if k == 0 else ()))
            stv = c.val("st")
            c.do("vector", lambda v: v.reduce_max(stats[:, 4:5], ps_r[:, 0:E_R], AX.X,
                                                  negate=True), waits=[("st", stv)])
            c.drain("vector")
            c.do("vector", lambda v: v.tensor_copy(sc[:, :], ps_r[:, 0:E_R]), inc="sv")
            c.do("scalar", lambda s: s.activation(sc[:, :], sc[:, :], AF.Exp,
                                                  bias=stats[:, 4:5]),
                 inc="ss", waits=[("sv", c.val("sv"))])
            c.do("vector", lambda v: v.reduce_sum(stats[:, 5:6], sc[:, :], AX.X),
                 waits=[("ss", c.val("ss"))])
            c.drain("vector")
            c.do("vector", lambda v: v.reciprocal(stats[:, 6:7], stats[:, 5:6]))
            c.drain("vector")
            c.do("vector", lambda v: v.tensor_scalar(
                sc[:, :], sc[:, :], stats[:, 6:7], None, AL.mult))
            c.drain("vector")
            c.do("vector", lambda v: v.tensor_tensor(gm[:, 0:4], sc[:, 0:16:4], sc[:, 1:16:4], AL.max))
            c.drain("vector")
            c.do("vector", lambda v: v.tensor_tensor(gm[:, 0:4], gm[:, 0:4], sc[:, 2:16:4], AL.max))
            c.drain("vector")
            c.do("vector", lambda v: v.tensor_tensor(gm[:, 0:4], gm[:, 0:4], sc[:, 3:16:4], AL.max))
            c.drain("vector")
            c.do("vector", lambda v: v.max(out=t8[:, :], in_=gm[:, 0:8]))
            c.drain("vector")
            c.do("vector", lambda v: v.tensor_scalar(gm[:, 0:4], gm[:, 0:4], t8[:, 1:2], None, AL.is_ge))
            c.drain("vector")
            c.do("vector", lambda v: v.tensor_scalar(gm[:, 0:4], gm[:, 0:4], 1e30, -1e30, AL.mult, AL.add))
            c.drain("vector")
            for g in range(4):
                c.do("vector", lambda v, g=g: v.tensor_scalar(
                    msk[:, 4 * g:4 * g + 4], sc[:, 4 * g:4 * g + 4], gm[:, g:g + 1], None, AL.add))
            c.drain("vector")
            c.do("vector", lambda v: v.max(out=t8[:, :], in_=msk[:, :]))
            c.drain("vector")
            c.do("vector", lambda v: v.tensor_scalar(msk[:, :], msk[:, :], t8[:, 3:4], None, AL.is_ge))
            c.drain("vector")
            c.do("vector", lambda v, i=i: v.scalar_tensor_tensor(
                comb[:, i, :], sc[:, :], 0.5, msk[:, :], AL.mult, AL.mult), inc="sv")

    def emit_xin(c, rb, dq, dst_slot, q="sync"):
        waits = [("stg", stg_ready["Xinit"]), (f"xo{rb}", c.val(f"xo{rb}"))]
        if c.slot_last[dst_slot] is not None:
            waits.append(c.slot_last[dst_slot])
        c.do(q, lambda s, rb=rb, dq=dq, sl=dst_slot: s.dma_start(
            xq[:, sl, :], X[rb * 128:(rb + 1) * 128, dq * 512:(dq + 1) * 512]),
            inc=f"xi{rb}", inc_n=16, waits=waits)
        return c.val(f"xi{rb}")

    def cast_triple(c, names):
        pairs = {"wg": (wgs, wg_in), "ws1": (ws1s, ws1_in), "ws3": (ws3s, ws3_in),
                 "ws2": (ws2s, ws2_in)}
        for name in names:
            if name in pairs:
                dst, srct = pairs[name]
                d, s = dst[:, :], srct[:, :]
            elif name.startswith("w1_"):
                ei = int(name[3:]); d = w1s[ei * INTER:(ei + 1) * INTER, :]; s = w1_in[ei * INTER:(ei + 1) * INTER, :]
            elif name.startswith("w3_"):
                ei = int(name[3:]); d = w3s[ei * INTER:(ei + 1) * INTER, :]; s = w3_in[ei * INTER:(ei + 1) * INTER, :]
            else:
                ei = int(name[3:]); d = w2s[ei * D:(ei + 1) * D, :]; s = w2_in[ei * D:(ei + 1) * D, :]
            c.do("gpsimd", lambda g, d=d, s=s: g.dma_start(d, s),
                 inc="stg", inc_n=16, waits=[("stg", c.val("stg"))])
            stg_ready[name] = c.val("stg")

    def script_expert(c, ft, e, nchunks, rbs, last_ffn, war_st, wq_war,
                      xin_q="sync", xout_q="sync", do_casts=False):
        if do_casts:
            if e < E_R:
                cast_triple(c, [f"w1_{e}", f"w3_{e}", f"w2_{e}"])
            elif e == E_R:
                cast_triple(c, ["ws1", "ws3", "ws2"])
        """h = silu(xh W1^T) * (xh W3^T); X[rb] += c_e * (h W2^T).
        rbs: X row-block indices receiving updates (t index i -> rbs[i]).
        war_st: tensor-count gate before overwriting W1T/W3T (prev expert h done)."""
        (w1T, w3T, w2q, h, s1, ps_h1, ps_h3, ps_y) = ft
        t_tot = 128 * len(rbs)
        if e < E_R:
            src1, src3 = w1s[e * INTER:(e + 1) * INTER, :], w3s[e * INTER:(e + 1) * INTER, :]
            stg_gate = max(stg_ready[f"w1_{e}"], stg_ready[f"w3_{e}"])
        else:
            j = e - E_R
            src1, src3 = ws1s[j * INTER:(j + 1) * INTER, :], ws3s[j * INTER:(j + 1) * INTER, :]
            stg_gate = max(stg_ready["ws1"], stg_ready["ws3"])
        for k in range(KC):
            c.do("sync", lambda s, k=k, src=src1: s.dma_start(
                w1T[:, k, :], src[:, k * 128:(k + 1) * 128], transpose=True),
                inc="wA", inc_n=16,
                waits=[("stg", stg_gate), ("st", war_st)] if k == 0 else ())
            c.do("sync", lambda s, k=k, src=src3: s.dma_start(
                w3T[:, k, :], src[:, k * 128:(k + 1) * 128], transpose=True),
                inc="wB", inc_n=16)
        wA_ready, wB_ready = c.val("wA"), c.val("wB")

        silu_done = [c.val("ss"), c.val("ss")]
        hmul_done = [c.val("sv"), c.val("sv")]
        it = 0
        for ici in range(IC):
            for (t0, n) in nchunks:
                par = it % 2
                for k in range(KC):
                    waits = []
                    if ici == 0 and t0 == 0 and k == 0:
                        waits = [("wA", wA_ready), ("wB", wB_ready), ("sv", c.val("sv"))]
                    if k == 0:
                        waits.append(("ss", silu_done[par]))
                    c.do("tensor", lambda t, k=k, ici=ici, t0=t0, n=n, par=par: t.matmul(
                        ps_h1[:, par, 0:n], w1T[:, k, ici * 128:(ici + 1) * 128],
                        xhT[:, k, t0:t0 + n], start=(k == 0), stop=(k == KC - 1)),
                        inc="st" if k == KC - 1 else None, waits=waits)
                st_h1 = c.val("st")
                c.do("scalar", lambda s, n=n, par=par: s.activation(
                    s1[:, par, 0:n], ps_h1[:, par, 0:n], AF.Silu), inc="ss",
                    waits=[("st", st_h1), ("sv", hmul_done[par])])
                silu_done[par] = c.val("ss")
                for k in range(KC):
                    waits = [("sv", hmul_done[par])] if k == 0 else []
                    c.do("tensor", lambda t, k=k, ici=ici, t0=t0, n=n, par=par: t.matmul(
                        ps_h3[:, par, 0:n], w3T[:, k, ici * 128:(ici + 1) * 128],
                        xhT[:, k, t0:t0 + n], start=(k == 0), stop=(k == KC - 1)),
                        inc="st" if k == KC - 1 else None, waits=waits)
                st_h3 = c.val("st")
                c.do("vector", lambda v, ici=ici, t0=t0, n=n, par=par: v.tensor_tensor(
                    h[:, ici, t0:t0 + n], s1[:, par, 0:n], ps_h3[:, par, 0:n], AL.mult),
                    inc="sv", waits=[("st", st_h3), ("ss", silu_done[par])])
                hmul_done[par] = c.val("sv")
                it += 1
        st_h_done = c.val("st")
        sv_h_done = c.val("sv")

        if e < E_R:
            w2src = w2s[e * D:(e + 1) * D, :]
            stg2 = stg_ready[f"w2_{e}"]
        else:
            j = e - E_R
            w2src = ws2s[:, j * INTER:(j + 1) * INTER]
            stg2 = stg_ready["ws2"]
        for dq in range(4):
            ring = dq % 2
            wqs = f"wq{ring}"
            for ici in range(IC):
                c.do("sync", lambda s, dq=dq, ici=ici, ring=ring, w2src=w2src: s.dma_start(
                    w2q[:, ring, ici, :],
                    w2src[dq * 512:(dq + 1) * 512, ici * 128:(ici + 1) * 128],
                    transpose=True),
                    inc=wqs, inc_n=16,
                    waits=[("stg", stg2), ("st", wq_war[ring])] if ici == 0 else ())
            wq_ready = c.val(wqs)
            for i, rb in enumerate(rbs):
                slot = c.rmw_idx % 4
                xi_v = emit_xin(c, rb, dq, slot, q=xin_q)
                pp = i % 2
                for ici in range(IC):
                    waits = []
                    if ici == 0:
                        waits = [(wqs, wq_ready),
                                 ("sv", c.val("sv") if (dq > 0 or i >= 2) else sv_h_done)]
                    c.do("tensor", lambda t, ici=ici, i=i, ring=ring, pp=pp: t.matmul(
                        ps_y[:, pp, :], h[:, ici, i * 128:(i + 1) * 128],
                        w2q[:, ring, ici, :], start=(ici == 0), stop=(ici == IC - 1)),
                        inc="st" if ici == IC - 1 else None, waits=waits)
                st_y = c.val("st")
                if e < E_R:
                    c.do("vector", lambda v, i=i, e=e, slot=slot, pp=pp: v.scalar_tensor_tensor(
                        xq[:, slot, :], ps_y[:, pp, :], comb[:, i, e:e + 1],
                        xq[:, slot, :], AL.mult, AL.add),
                        inc="sv", waits=[("st", st_y), (f"xi{rb}", xi_v)])
                else:
                    c.do("vector", lambda v, slot=slot, pp=pp: v.scalar_tensor_tensor(
                        xq[:, slot, :], ps_y[:, pp, :], 0.5,
                        xq[:, slot, :], AL.mult, AL.add),
                        inc="sv", waits=[("st", st_y), (f"xi{rb}", xi_v)])
                sv_y = c.val("sv")
                if last_ffn and e == E_ALL - 1:
                    dst = y_out[(rb - 1) * 128:rb * 128, dq * 512:(dq + 1) * 512]
                else:
                    dst = X[rb * 128:(rb + 1) * 128, dq * 512:(dq + 1) * 512]
                c.do(xout_q, lambda s, dst=dst, slot=slot: s.dma_start(dst, xq[:, slot, :]),
                     inc=f"xo{rb}", inc_n=16, waits=[("sv", sv_y)])
                c.slot_last[slot] = (f"xo{rb}", c.val(f"xo{rb}"))
                c.rmw_idx += 1
            wq_war[ring] = c.val("st")
        return st_h_done

    def script_ffn(c, ft, rt, nchunks, rbs, last_ffn, xin_q="sync",
                   xout_q="sync", first=False):
        if first:
            for k in range(KC):
                c.do("sync", lambda s, k=k: s.dma_start(
                    wgT[:, k, :], wgs[:, k * 128:(k + 1) * 128], transpose=True),
                    inc="wg", inc_n=16,
                    waits=[("stg", stg_ready["wg"])] if k == 0 else ())
        script_router(c, rt, len(rbs))
        war = [c.val("st")]
        wq_war = [c.val("st"), c.val("st")]
        for e in range(E_ALL):
            war[0] = script_expert(c, ft, e, nchunks, rbs, last_ffn, war[0],
                                   wq_war, xin_q, xout_q, do_casts=first)

    def script_attn(c, at):
        (wo_sb, qkv, qT, kT, vTt, vtm, amask, attnT, ssc, psm, pbf,
         ps_q, ps_s, ps_pt, ps_vt, ps_o, ps_y) = at
        entry_st = c.val("st")
        entry_sv = c.val("sv")
        c.do("scalar", lambda s: s.dma_start(
            amask[:, :, :], masks_in.rearrange("p (q w) -> p q w", q=8)),
            inc="mk", inc_n=16, waits=[("st", entry_st), ("sv", entry_sv)])
        for k in range(KC):
            c.do("gpsimd", lambda g, k=k: g.dma_start(
                wo_sb[:, k, :], wo_in[k * 128:(k + 1) * 128, :]),
                inc="wo", inc_n=16,
                waits=[("stg", c.val("stg")), ("st", entry_st), ("sv", entry_sv)] if k == 0 else ())
        wo_ready = c.val("wo")
        war_qkv = [entry_st, entry_st]
        st_attn_done_by_head = []
        for hh in range(NH):
            ring = hh % 2
            aws = f"aw{ring}"
            for wi, wsrc in enumerate((wq_in, wk_in, wv_in)):
                c.do("gpsimd", lambda g, wi=wi, wsrc=wsrc, hh=hh, ring=ring: g.dma_start(
                    qkv[:, ring, wi, :, :],
                    wsrc.rearrange("(k p) f -> p k f", p=128)[:, :, hh * 128:(hh + 1) * 128]),
                    inc=aws, inc_n=16,
                    waits=[("st", war_qkv[ring])] if wi == 0 else ())
            aw_ready = c.val(aws)
            def proj(widx, dst, chunks, toff_src, scale, sv_prev):
                for (t0, n) in chunks:
                    for k in range(KC):
                        waits = []
                        if k == 0:
                            waits = [(aws, aw_ready), ("ss", c.val("ss"))]
                        c.do("tensor", lambda t, k=k, widx=widx, t0=t0, n=n, ring=ring: t.matmul(
                            ps_q[:, 0:n], qkv[:, ring, widx, k, :],
                            xhT[:, k, toff_src + t0: toff_src + t0 + n],
                            start=(k == 0), stop=(k == KC - 1)),
                            inc="st" if k == KC - 1 else None, waits=waits)
                    stv = c.val("st")
                    if scale is None:
                        c.do("scalar", lambda s, t0=t0, n=n, dst=dst: s.copy(
                            dst[:, t0:t0 + n], ps_q[:, 0:n]), inc="ss", waits=[("st", stv)])
                    else:
                        c.do("scalar", lambda s, t0=t0, n=n, dst=dst: s.activation(
                            dst[:, t0:t0 + n], ps_q[:, 0:n], AF.Copy, scale=scale),
                            inc="ss", waits=[("st", stv)])
            proj(0, qT, [(0, 512), (512, 512)], HALO, QSCALE, 0)
            proj(1, kT, [(0, 512), (512, 512), (1024, 256)], 0, None, 0)
            proj(2, vTt, [(0, 512), (512, 512), (1024, 256)], 0, None, 0)
            ss_proj = c.val("ss")
            for tv in range(RB_FULL):
                pp = tv % 2
                c.do("tensor", lambda t, tv=tv, pp=pp: t.transpose(
                    ps_vt[:, pp, 0:128], vTt[:, tv * 128:(tv + 1) * 128], ident[:, :]),
                    inc="st",
                    waits=[("ss", ss_proj), ("sv", c.val("sv"))] if tv == 0 else
                          ([("sv", c.val("sv"))] if tv >= 2 else []))
                stv = c.val("st")
                c.do("vector", lambda v, tv=tv, pp=pp: v.tensor_copy(
                    vtm[:, tv, :], ps_vt[:, pp, 0:128]), inc="sv", waits=[("st", stv)])
            for qc in range(RB_OWN):
                c.do("tensor", lambda t, qc=qc: t.matmul(
                    ps_s[:, 0:KW], qT[:, qc * 128:(qc + 1) * 128],
                    kT[:, qc * 128:qc * 128 + KW], start=True, stop=True),
                    inc="st", waits=[("sv", c.val("sv")), ("mk", 16)])
                st_s = c.val("st")
                c.do("vector", lambda v, qc=qc: v.tensor_tensor(
                    ssc[:, :], ps_s[:, 0:KW], amask[:, qc, :], AL.add),
                    waits=[("st", st_s), ("ss", c.val("ss"))])
                c.drain("vector")
                c.do("vector", lambda v: v.reduce_max(stats[:, 4:5], ssc[:, :], AX.X,
                                                      negate=True), inc="sv")
                c.do("scalar", lambda s: s.activation(psm[:, :], ssc[:, :], AF.Exp,
                                                      bias=stats[:, 4:5]),
                     inc="ss", waits=[("sv", c.val("sv"))])
                c.do("vector", lambda v: v.reduce_sum(stats[:, 5:6], psm[:, :], AX.X),
                     waits=[("ss", c.val("ss"))])
                c.drain("vector")
                c.do("vector", lambda v: v.reciprocal(stats[:, 6:7], stats[:, 5:6]))
                c.drain("vector")
                c.do("vector", lambda v: v.tensor_scalar(
                    pbf[:, :], psm[:, :], stats[:, 6:7], None, AL.mult), inc="sv")
                sv_pb = c.val("sv")
                for j in range(3):
                    c.do("tensor", lambda t, j=j: t.transpose(
                        ps_pt[:, j * 128:(j + 1) * 128], pbf[:, j * 128:(j + 1) * 128],
                        ident[:, :]), inc="st",
                        waits=[("sv", sv_pb)] if j == 0 else [])
                st_pt = c.val("st")
                c.do("vector", lambda v: v.tensor_copy(pbf[:, :], ps_pt[:, 0:KW]),
                     inc="sv", waits=[("st", st_pt)])
                sv_pt = c.val("sv")
                for j in range(3):
                    c.do("tensor", lambda t, j=j, qc=qc: t.matmul(
                        ps_o[:, :], vtm[:, qc + j, :], pbf[:, j * 128:(j + 1) * 128],
                        start=(j == 0), stop=(j == 2)),
                        inc="st" if j == 2 else None,
                        waits=[("sv", sv_pt), ("ss", c.val("ss"))] if j == 0 else [])
                st_o = c.val("st")
                c.do("scalar", lambda s, hh=hh, qc=qc: s.copy(
                    attnT[:, hh, qc * 128:(qc + 1) * 128], ps_o[:, :]),
                    inc="ss", waits=[("st", st_o)])
            war_qkv[ring] = c.val("st")
            st_attn_done_by_head.append(c.val("st"))
        ss_attn_done = c.val("ss")
        for dq in range(4):
            for i in range(RB_OWN):
                rb = i + 1
                slot = c.rmw_idx % 4
                xi_v = emit_xin(c, rb, dq, slot)
                pp = i % 2
                for k in range(KC):
                    waits = []
                    if k == 0:
                        waits = [("wo", wo_ready), ("ss", ss_attn_done),
                                 ("sv", c.val("sv") if (dq > 0 or i >= 2) else 0)]
                    c.do("tensor", lambda t, k=k, i=i, dq=dq, pp=pp: t.matmul(
                        ps_y[:, pp, :], attnT[:, k, i * 128:(i + 1) * 128],
                        wo_sb[:, k, dq * 512:(dq + 1) * 512],
                        start=(k == 0), stop=(k == KC - 1)),
                        inc="st" if k == KC - 1 else None, waits=waits)
                st_y = c.val("st")
                c.do("vector", lambda v, slot=slot, pp=pp: v.scalar_tensor_tensor(
                    xq[:, slot, :], ps_y[:, pp, :], 1.0, xq[:, slot, :],
                    AL.mult, AL.add), inc="sv", waits=[("st", st_y), (f"xi{rb}", xi_v)])
                sv_y = c.val("sv")
                c.do("sync", lambda s, rb=rb, dq=dq, slot=slot: s.dma_start(
                    X[rb * 128:(rb + 1) * 128, dq * 512:(dq + 1) * 512], xq[:, slot, :]),
                    inc=f"xo{rb}", inc_n=16, waits=[("sv", sv_y)])
                c.slot_last[slot] = (f"xo{rb}", c.val(f"xo{rb}"))
                c.rmw_idx += 1

    def script_dump_X(c):
        c.do("sync", lambda s: s.dma_start(y_out[:, :], X[HALO:HALO + T_OWN, :]),
             inc="fin", inc_n=16,
             waits=[(f"xo{r}", c.val(f"xo{r}")) for r in range(RB_FULL)])
        c.wait("sync", "fin", c.val("fin"))

    def script_finish(c):
        for r in range(1, 1 + RB_OWN):
            c.wait("sync", f"xo{r}", c.val(f"xo{r}"))

    run_phase(script_prologue)

    ln_ctx = []

    def alloc(name, shape, dtype):
        cm = nc.sbuf_tensor(name, shape, dtype)
        t = cm.__enter__()
        ln_ctx.append((cm, t))
        return t

    def alloc_ps(name, shape, dtype):
        cm = nc.psum_tensor(name, shape, dtype)
        t = cm.__enter__()
        ln_ctx.append((cm, t))
        return t

    def free_all():
        while ln_ctx:
            cm, _ = ln_ctx.pop()
            cm.__exit__(None, None, None)

    def ln_alloc(tag):
        return (alloc(f"xbuf{tag}", [128, 2, D], F32),
                alloc(f"sq{tag}", [128, D], F32),
                alloc(f"xh_tm{tag}", [128, D], BF16),
                alloc_ps(f"ps_t{tag}", [128, 2, 1024], BF16))

    def ffn_alloc(tag):
        ft = (alloc(f"w1T{tag}", [128, KC, INTER], BF16),
              alloc(f"w3T{tag}", [128, KC, INTER], BF16),
              alloc(f"w2q{tag}", [128, 2, IC, 512], BF16),
              alloc(f"h{tag}", [128, IC, T_FULL], BF16),
              alloc(f"s1{tag}", [128, 2, 512], F32),
              alloc_ps(f"ps_h1{tag}", [128, 2, 512], F32),
              alloc_ps(f"ps_h3{tag}", [128, 2, 512], F32),
              alloc_ps(f"ps_y{tag}", [128, 2, 512], F32))
        rt = (alloc_ps(f"ps_r{tag}", [128, 512], F32),
              alloc(f"rsc{tag}", [128, E_R], F32),
              alloc(f"rgm{tag}", [128, 8], F32),
              alloc(f"rt8{tag}", [128, 8], F32),
              alloc(f"rmsk{tag}", [128, E_R], F32))
        return ft, rt

    t_ln = ln_alloc("a")
    run_phase(lambda c: script_ln(c, t_ln, list(range(RB_FULL)), 0))
    free_all()
    ft, rt = ffn_alloc("a")

    run_phase(lambda c: script_ffn(c, ft, rt, [(0, 512), (512, 512), (1024, 256)],
                                   list(range(RB_FULL)), last_ffn=False,
                                   xin_q="scalar", xout_q="gpsimd", first=True))
    free_all()
    if n_sub == 1:
        run_phase(script_dump_X)
        for cm in reversed(sem_ctx):
            cm.__exit__(None, None, None)
        for cm in reversed(pers_ctx):
            cm.__exit__(None, None, None)
        return nc

    t_ln = ln_alloc("b")
    run_phase(lambda c: script_ln(c, t_ln, list(range(RB_FULL)), 0))
    free_all()
    at = (alloc("wo_sb", [128, KC, D], BF16),
          alloc("qkv", [128, 2, 3, KC, 128], BF16),
          alloc("qT", [128, T_OWN], BF16),
          alloc("kT", [128, T_FULL], BF16),
          alloc("vTt", [128, T_FULL], BF16),
          alloc("vtm", [128, RB_FULL, 128], BF16),
          alloc("amask", [128, 8, KW], F32),
          alloc("attnT", [128, NH, T_OWN], BF16),
          alloc("ssc", [128, KW], F32),
          alloc("psm", [128, KW], F32),
          alloc("pbf", [128, KW], BF16),
          alloc_ps("aps_q", [128, 512], F32),
          alloc_ps("aps_s", [128, 512], F32),
          alloc_ps("aps_pt", [128, KW], BF16),
          alloc_ps("aps_vt", [128, 2, 1024], BF16),
          alloc_ps("aps_o", [128, 128], F32),
          alloc_ps("aps_y", [128, 2, 512], F32))
    run_phase(lambda c: script_attn(c, at))
    free_all()
    if n_sub == 2:
        run_phase(script_dump_X)
        for cm in reversed(sem_ctx):
            cm.__exit__(None, None, None)
        for cm in reversed(pers_ctx):
            cm.__exit__(None, None, None)
        return nc

    t_ln = ln_alloc("d")
    run_phase(lambda c: script_ln(c, t_ln, list(range(1, 1 + RB_OWN)), 0))
    free_all()
    ft, rt = ffn_alloc("d")
    run_phase(lambda c: script_ffn(c, ft, rt, [(0, 512), (512, 512)],
                                   list(range(1, 1 + RB_OWN)), last_ffn=True,
                                   xin_q="scalar", xout_q="gpsimd"))
    free_all()
    run_phase(script_finish)

    for cm in reversed(sem_ctx):
        cm.__exit__(None, None, None)
    for cm in reversed(pers_ctx):
        cm.__exit__(None, None, None)
    return nc


def _build_mask(qstart):
    S = 4096
    m = np.full((8, 128, KW), NEG, dtype=np.float32)
    for qc in range(8):
        gq = qstart + qc * 128 + np.arange(128)
        gk = qstart - HALO + qc * 128 + np.arange(KW)
        valid = (gk >= 0) & (gk < S)
        band = np.abs(gq[:, None] // 64 - gk[None, :] // 64) <= 2
        m[qc] = np.where(valid[None, :] & band, 0.0, NEG)
    return np.ascontiguousarray(m.transpose(1, 0, 2).reshape(128, 8 * KW))


def kernel(L0_attn_hidden_states, L0_attn_Wq, L0_attn_Wk, L0_attn_Wv, L0_attn_Wo,
           L0_ffn_Wg, L0_ffn_W1, L0_ffn_W3, L0_ffn_W2, L0_ffn_Ws1, L0_ffn_Ws3,
           L0_ffn_Ws2, _n_sub=3):
    x = np.asarray(L0_attn_hidden_states, dtype=np.float32)
    B, S, _ = x.shape
    w = {
        "Wq": np.ascontiguousarray(L0_attn_Wq, dtype=np.float32),
        "Wk": np.ascontiguousarray(L0_attn_Wk, dtype=np.float32),
        "Wv": np.ascontiguousarray(L0_attn_Wv, dtype=np.float32),
        "Wo": np.ascontiguousarray(L0_attn_Wo, dtype=np.float32),
        "Wg": np.ascontiguousarray(L0_ffn_Wg, dtype=np.float32),
        "W1": np.ascontiguousarray(np.asarray(L0_ffn_W1, dtype=np.float32).reshape(E_R * INTER, D)),
        "W3": np.ascontiguousarray(np.asarray(L0_ffn_W3, dtype=np.float32).reshape(E_R * INTER, D)),
        "W2": np.ascontiguousarray(np.asarray(L0_ffn_W2, dtype=np.float32).reshape(E_R * D, INTER)),
        "Ws1": np.ascontiguousarray(L0_ffn_Ws1, dtype=np.float32),
        "Ws3": np.ascontiguousarray(L0_ffn_Ws3, dtype=np.float32),
        "Ws2": np.ascontiguousarray(L0_ffn_Ws2, dtype=np.float32),
    }
    in_maps = []
    for core in range(8):
        b, q = core // 4, core % 4
        qstart = q * T_OWN
        xs = np.zeros((T_FULL, D), dtype=np.float32)
        lo, hi = qstart - HALO, qstart + T_OWN + HALO
        clo, chi = max(lo, 0), min(hi, S)
        xs[clo - lo:chi - lo] = x[b, clo:chi]
        m = dict(w)
        m["x"] = xs
        m["masks"] = _build_mask(qstart)
        m["identb"] = np.eye(128, dtype=ml_dtypes.bfloat16)
        in_maps.append(m)
    nc = build_nc(n_sub=_n_sub)
    res = run_bass_kernel_spmd(nc, in_maps, core_ids=list(range(8)))
    out = np.empty((B, S, D), dtype=np.float32)
    for core in range(8):
        b, q = core // 4, core % 4
        out[b, q * T_OWN:(q + 1) * T_OWN] = res.results[core]["y"]
    return out
